# version 05 (final)
# baseline (speedup 1.0000x reference)
import numpy as np
import concourse.bass as bass
import concourse.mybir as mybir
from concourse.bass_utils import run_bass_kernel_spmd
from concourse.alu_op_type import AluOpType
from contextlib import ExitStack

F32 = mybir.dt.float32
BF16 = mybir.dt.bfloat16
AF = mybir.ActivationFunctionType
MUL, ADD, SUB, MAXOP = AluOpType.mult, AluOpType.add, AluOpType.subtract, AluOpType.max

NCORES = 8
SEQS = 2
T = 2048
C = 1024
NH = 16
D = 64
NG = 4
GH = NH // NG
NPAIR = 2
NJC = T // 128
NIT = 4
NCC = C // 128
QK_SCALE = float(D) ** (-0.25)
EPS = 1e-5
RING = 3
BMARG_M, BMARG_A = 1.001, 0.01

A_ONES, A_IDENT, A_BOUTB = 0, 128, 256
A_MUR, A_MUK, A_OMUR, A_OMUK = 1280, 1288, 1296, 1304
A_NEGB = 1312
A_K2M, A_Q2M, A_T16, A_TROW = 1344, 1352, 1372, 1392
A_SOH, A_SOL = 2048, 10240
C_WV2H, C_WV2L = 2048, 6144
C_XH, C_XL = 10240, 11264
C_KAH, C_KAL = 12288, 13312
C_RST = 14336
C_Y1TILE, C_X2TILE = 16384, 17408
A_X1T = 18432
A_SKHI, A_SKLO = 18432, 20480
A_SQHI, A_SQLO = 22528, 24576
A_SV = 26624
A_RING = 30784
A_OSTG = 33856
A_X0HT, A_X0LT = 34824, 36872
A_WQH, A_WQL = 38920, 39944
A_WKH, A_WKL = 40968, 41992
A_WVH, A_WVL = 43016, 44040
A_SQSC = 45064
A_X0TILE = 45576
B_WOH, B_WOL = 34824, 38920
B_Y0, B_X1TILE, B_SQSC = 43016, 44040, 45064
B_X0TILE = 46088
B_LN = 48136
C_WRH, C_WRL, C_WK2H, C_WK2L = 34824, 38920, 43016, 47112
S_SWF = 51208
S_SWFB = 51720
S_MIX = 52744
S_LN = 53000
ARENA = 53208

ENGS = ("sync", "tensor", "scalar", "vector", "gpsimd")
STOP = "full"
SKIP = set()


class Plan:
    def __init__(self):
        self.ops = {e: [] for e in ENGS}

    def emit(self, eng, fn):
        self.ops[eng].append(fn)

    def run(self, block):
        for eng in ENGS:
            ops = self.ops[eng]
            if not ops:
                continue
            def make(eng_ops):
                def body(e):
                    for fn in eng_ops:
                        fn(e)
                return body
            getattr(block, eng)(make(ops))


class Sem:
    def __init__(self, handle):
        self.h = handle
        self.n = 0

    def inc(self, plan, eng, emit_fn, amount=1):
        self.n += amount
        def fn(e, emit_fn=emit_fn, amount=amount):
            emit_fn(e).then_inc(self.h, amount)
        plan.emit(eng, fn)
        return self.n

    def wait(self, plan, eng, value):
        if value and value > 0:
            plan.emit(eng, lambda e, v=value: e.wait_ge(self.h, v))


def build_nc():
    nc = bass.Bass("TRN2", target_bir_lowering=False,
                   detect_race_conditions=False)
    x_d = nc.dram_tensor("x", [SEQS * T, C], F32, kind="ExternalInput").ap()
    w_d = {n: nc.dram_tensor(n, [C, C], F32, kind="ExternalInput").ap()
           for n in ["wqT", "wkT", "wvT", "woT", "wrT", "wk2T", "wv2T"]}
    bout_d = nc.dram_tensor("bout", [1, C], F32, kind="ExternalInput").ap()
    mus_d = nc.dram_tensor("mus", [128, 16], F32, kind="ExternalInput").ap()
    consts_d = nc.dram_tensor("consts", [128, 256], F32, kind="ExternalInput").ap()
    out_d = nc.dram_tensor("out", [SEQS * T, C], F32, kind="ExternalOutput").ap()

    with ExitStack() as stack:
        ar_t = stack.enter_context(nc.sbuf_tensor("arena", [128, ARENA], F32))
        pS0 = stack.enter_context(nc.psum_tensor("pS0", [128, 1024], F32))
        pS1 = stack.enter_context(nc.psum_tensor("pS1", [128, 1024], F32))
        pVe = stack.enter_context(nc.psum_tensor("pVe", [65, 512], F32))
        pVo = stack.enter_context(nc.psum_tensor("pVo", [65, 512], F32))
        pB = stack.enter_context(nc.psum_tensor("pB", [128, 1024], F32))
        d_in_h = stack.enter_context(nc.semaphore("d_in"))
        d_gp_h = stack.enter_context(nc.semaphore("d_gp"))
        ts_h = stack.enter_context(nc.semaphore("ts"))
        ss_h = stack.enter_context(nc.semaphore("ss"))
        vs_h = stack.enter_context(nc.semaphore("vs"))
        block = stack.enter_context(nc.Block())

        def f32(off, n, p0=0, p1=128):
            return ar_t[p0:p1, off:off + n]

        def bf(off, n, p0=0, p1=128):
            return ar_t[p0:p1, off:off + n // 2].bitcast(BF16)

        plan = Plan()
        d_in, d_gp = Sem(d_in_h), Sem(d_gp_h)
        ts, ss, vs = Sem(ts_h), Sem(ss_h), Sem(vs_h)

        def dma_in(dst, src):
            d_in.wait(plan, "sync", d_in.n)
            return d_in.inc(plan, "sync", lambda e, d=dst, s=src: e.dma_start(out=d, in_=s), 16)

        def dma_gp(fn):
            d_gp.wait(plan, "gpsimd", d_gp.n)
            return d_gp.inc(plan, "gpsimd", fn, 16)

        def mm(out, lhsT, rhs, start, stop, tp=None, sem=None):
            fn = lambda e, o=out, l=lhsT, r=rhs, a=start, z=stop, tp=tp: \
                e.matmul(o, l, r, start=a, stop=z, tile_position=tp)
            if sem is not None:
                return sem.inc(plan, "tensor", fn)
            plan.emit("tensor", fn)
            return None

        ones = f32(A_ONES, 128)
        ident = f32(A_IDENT, 128)
        ring = [f32(A_RING + b * 1024, 1024) for b in range(RING)]

        dma_in(f32(A_ONES, 256), consts_d)
        dma_in(f32(A_MUR, 16), mus_d)
        c_ld = dma_in(f32(A_TROW, 1024, 0, 1), bout_d)
        d_in.wait(plan, "vector", c_ld)
        v_mu = vs.inc(plan, "vector", lambda e: e.tensor_scalar(
            f32(A_OMUR, 16), f32(A_MUR, 16), -1.0, 1.0, op0=MUL, op1=ADD))
        d_in.wait(plan, "tensor", c_ld)
        mm(pB[:, 0:512], ones[0:1, 0:128], f32(A_TROW, 512, 0, 1), True, True)
        t_bb = mm(pB[:, 512:1024], ones[0:1, 0:128], f32(A_TROW + 512, 512, 0, 1), True, True, sem=ts)
        ts.wait(plan, "scalar", t_bb)
        s_bb = ss.inc(plan, "scalar", lambda e: e.activation(
            f32(A_BOUTB, 1024), pB[:, 0:1024], AF.Copy, bias=0.0, scale=1.0))

        dep = {"ts": t_bb, "ss": s_bb, "vs": v_mu, "dgp": 0}

        def split_weight(dst_h, dst_l, src_fn, nchunks, chw, boff, scale, first_waits):
            lo_counts = []
            for k in range(nchunks):
                b = boff + (k % 2) * chw
                if k < 2:
                    for sem, cnt in first_waits.items():
                        sem.wait(plan, "sync", cnt)
                else:
                    vs.wait(plan, "sync", lo_counts[k - 2])
                ld = dma_in(f32(b, chw), src_fn(k))
                d_in.wait(plan, "scalar", ld)
                h = ss.inc(plan, "scalar", lambda e, b=b, k=k: e.activation(
                    bf(dst_h + k * chw // 2, chw), f32(b, chw), AF.Copy, bias=0.0, scale=scale))
                ss.wait(plan, "vector", h)
                l = vs.inc(plan, "vector", lambda e, b=b, k=k: e.scalar_tensor_tensor(
                    bf(dst_l + k * chw // 2, chw), f32(b, chw), scale,
                    bf(dst_h + k * chw // 2, chw), op0=MUL, op1=SUB))
                lo_counts.append(l)
            return h, l

        def proj3(psum_ap, w_h_off, w_l_off, wslice_fn, x_h, x_l, bank_first):
            entries = []
            for cc in range(NCC):
                entries.append((wslice_fn(w_h_off, cc), x_h(cc)))
                entries.append((wslice_fn(w_h_off, cc), x_l(cc)))
                entries.append((wslice_fn(w_l_off, cc), x_h(cc)))
            last = None
            for i, (l, r) in enumerate(entries):
                st_f = bank_first and (i == 0)
                if i < len(entries) - 1:
                    mm(psum_ap, l, r, st_f, False)
                else:
                    last = mm(psum_ap, l, r, st_f, True, sem=ts)
            return last

        def ln_norm(y_ap, sq_ap, lncol, out_ap, y_ready_v):
            vs.wait(plan, "scalar", y_ready_v)
            s_sq = ss.inc(plan, "scalar", lambda e: e.activation(
                sq_ap, y_ap, AF.Square, bias=0.0, scale=1.0))
            v_red = vs.inc(plan, "vector", lambda e: e.tensor_reduce(
                f32(lncol + 1, 1), y_ap, axis=mybir.AxisListType.X, op=ADD))
            ss.wait(plan, "vector", s_sq)
            v_red2 = vs.inc(plan, "vector", lambda e: e.tensor_reduce(
                f32(lncol, 1), sq_ap, axis=mybir.AxisListType.X, op=ADD))
            vs.wait(plan, "vector", max(v_red, v_red2))
            plan.emit("vector", lambda e: e.tensor_scalar(
                f32(lncol + 2, 1), f32(lncol + 1, 1), -1.0 / C, 0.0, op0=MUL, op1=ADD))
            plan.emit("vector", lambda e: e.tensor_scalar(
                f32(lncol + 3, 1), f32(lncol, 1), 1.0 / C, 0.0, op0=MUL, op1=ADD))
            plan.emit("vector", lambda e: e.tensor_tensor(
                f32(lncol + 4, 1), f32(lncol + 2, 1), f32(lncol + 2, 1), op=MUL))
            plan.emit("vector", lambda e: e.tensor_tensor(
                f32(lncol + 5, 1), f32(lncol + 3, 1), f32(lncol + 4, 1), op=SUB))
            plan.emit("vector", lambda e: e.tensor_scalar(
                f32(lncol + 5, 1), f32(lncol + 5, 1), 1.0, EPS, op0=MUL, op1=ADD))
            v_rec = vs.inc(plan, "vector", lambda e: e.reciprocal(
                f32(lncol + 6, 1), f32(lncol + 5, 1)))
            vs.wait(plan, "scalar", v_rec)
            s_inv = ss.inc(plan, "scalar", lambda e: e.activation(
                f32(lncol + 6, 1), f32(lncol + 6, 1), AF.Sqrt, bias=0.0, scale=1.0))
            ss.wait(plan, "vector", s_inv)
            v_out = vs.inc(plan, "vector", lambda e: e.tensor_scalar(
                out_ap, y_ap, f32(lncol + 2, 1), f32(lncol + 6, 1), op0=ADD, op1=MUL))
            return s_inv, v_out

        def stage_A1(s, g):
            colsl = slice(g * 256, (g + 1) * 256)
            fw = {ts: dep["ts"], ss: dep["ss"], vs: dep["vs"]}
            wq_s, wq_v = split_weight(A_WQH, A_WQL,
                                      lambda k: w_d["wqT"][k * 128:(k + 1) * 128, colsl],
                                      NCC, 256, S_SWF, QK_SCALE, fw)
            wk_s, wk_v = split_weight(A_WKH, A_WKL,
                                      lambda k: w_d["wkT"][k * 128:(k + 1) * 128, colsl],
                                      NCC, 256, S_SWF, QK_SCALE, {vs: wq_v})
            wv_s, wv_v = split_weight(A_WVH, A_WVL,
                                      lambda k: w_d["wvT"][k * 128:(k + 1) * 128, colsl],
                                      NCC, 256, S_SWF, 1.0, {vs: wk_v})
            vs.wait(plan, "vector", dep["vs"])
            sv_view = f32(A_SV, 4160).rearrange("p (j h c) -> p j h c", j=NJC, h=GH)
            v_ones = vs.inc(plan, "vector", lambda e: e.memset(sv_view[:, :, :, 64:65], 1.0))

            x0h = bf(A_X0HT, 4096).rearrange("p (c n) -> p c n", c=NCC)
            x0l = bf(A_X0LT, 4096).rearrange("p (c n) -> p c n", c=NCC)
            wsl = {off: bf(off, 2048).rearrange("p (c h) -> p c h", c=NCC)
                   for off in (A_WQH, A_WQL, A_WKH, A_WKL, A_WVH, A_WVL)}

            if STOP == "a1w":
                return None

            x0_tr_last = dep["ts"]
            split_s, split_v = dep["ss"], dep["vs"]
            evac_v = {"pS0": [dep["vs"], dep["vs"]], "pS1": [dep["vs"], dep["vs"]]}
            evac_s = {"pS0": [dep["ss"], dep["ss"]], "pS1": [dep["ss"], dep["ss"]]}
            stat_v_last = dep["vs"]
            sqsc_ts = dep["ts"]

            for st in range(4):
                for nn in range(4):
                    ts.wait(plan, "sync", x0_tr_last)
                    if st == 0 and nn == 0:
                        vs.wait(plan, "sync", dep["vs"])
                    row = s * T + st * 512 + nn * 128
                    ld = dma_in(f32(A_X0TILE, 1024), x_d[row:row + 128, :])
                    d_in.wait(plan, "tensor", ld)
                    ss.wait(plan, "tensor", split_s)
                    vs.wait(plan, "tensor", split_v)
                    if nn == 0:
                        vs.wait(plan, "tensor", max(evac_v["pS1"]))
                    if st == 0 and nn == 0 and g == 0:
                        ts.wait(plan, "tensor", dep["ts"])
                    for cc in range(NCC):
                        sem = ts if cc == NCC - 1 else None
                        r = mm(pS1[:, cc * 128:(cc + 1) * 128],
                               f32(A_X0TILE + cc * 128, 128), ident, True, True, sem=sem)
                        if sem:
                            x0_tr_last = r
                    ts.wait(plan, "scalar", x0_tr_last)
                    srcv = pS1[:, 0:1024].rearrange("p (c n) -> p c n", c=NCC)
                    split_s = ss.inc(plan, "scalar", lambda e, nn=nn, srcv=srcv: e.activation(
                        x0h[:, :, nn * 128:(nn + 1) * 128], srcv, AF.Copy, bias=0.0, scale=1.0))
                    ss.wait(plan, "vector", split_s)
                    split_v = vs.inc(plan, "vector", lambda e, nn=nn, srcv=srcv:
                                     e.scalar_tensor_tensor(
                                         x0l[:, :, nn * 128:(nn + 1) * 128], srcv, 1.0,
                                         x0h[:, :, nn * 128:(nn + 1) * 128], op0=MUL, op1=SUB))
                if STOP == "a1x":
                    return None
                if st == 0:
                    ss.wait(plan, "tensor", wv_s)
                    vs.wait(plan, "tensor", max(wv_v, v_ones))
                ss.wait(plan, "tensor", split_s)
                vs.wait(plan, "tensor", split_v)

                k_done = []
                for p in range(NPAIR):
                    vs.wait(plan, "tensor", evac_v["pS0"][p])
                    ss.wait(plan, "tensor", evac_s["pS0"][p])
                    half = pS0[:, p * 512:(p + 1) * 512]
                    t_k = proj3(half, A_WKH, A_WKL,
                                lambda off, cc, p=p: wsl[off][:, cc, p * 128:(p + 1) * 128],
                                lambda cc: x0h[:, cc, :], lambda cc: x0l[:, cc, :], True)
                    k_done.append(t_k)
                if STOP == "a1kp":
                    return None
                for p in range(NPAIR):
                    ts.wait(plan, "scalar", max(k_done[p], sqsc_ts))
                    cslice = slice(p * 2048 + st * 512, p * 2048 + st * 512 + 512)
                    s_sq = ss.inc(plan, "scalar", lambda e, p=p: e.activation(
                        f32(A_SQSC, 512), pS0[:, p * 512:(p + 1) * 512],
                        AF.Square, bias=0.0, scale=1.0))
                    s_hi = ss.inc(plan, "scalar", lambda e, p=p, cslice=cslice: e.activation(
                        bf(A_SKHI, 4096)[:, cslice], pS0[:, p * 512:(p + 1) * 512],
                        AF.Copy, bias=0.0, scale=1.0))
                    evac_s["pS0"][p] = s_hi
                    ss.wait(plan, "vector", s_hi)
                    v_lo = vs.inc(plan, "vector", lambda e, p=p, cslice=cslice:
                                  e.scalar_tensor_tensor(
                                      bf(A_SKLO, 4096)[:, cslice],
                                      pS0[:, p * 512:(p + 1) * 512],
                                      1.0, bf(A_SKHI, 4096)[:, cslice],
                                      op0=MUL, op1=SUB))
                    evac_v["pS0"][p] = v_lo
                    if STOP in ("a1ke", "a1ke0"):
                        continue
                    ss.wait(plan, "tensor", s_sq)
                    vs.wait(plan, "tensor", stat_v_last)
                    mm(pB[0:1, 0:512], ones[0:64, 0:1], f32(A_SQSC, 512, 0, 64), True, True)
                    t_st = mm(pB[0:1, 512:1024], ones[64:128, 1:2], f32(A_SQSC, 512, 64, 128),
                              True, True, sem=ts)
                    sqsc_ts = t_st
                    ts.wait(plan, "vector", t_st)
                    for par in range(2):
                        hidx = 2 * p + par
                        src = pB[0:1, par * 512:par * 512 + 512]
                        if st == 0:
                            stat_v_last = vs.inc(
                                plan, "vector", lambda e, hidx=hidx, src=src: e.reduce_max(
                                    f32(A_K2M + hidx, 1, 0, 1), src,
                                    axis=mybir.AxisListType.X))
                        else:
                            r1 = vs.inc(
                                plan, "vector", lambda e, hidx=hidx, src=src: e.reduce_max(
                                    f32(A_T16 + hidx, 1, 0, 1), src,
                                    axis=mybir.AxisListType.X))
                            vs.wait(plan, "vector", r1)
                            stat_v_last = vs.inc(
                                plan, "vector", lambda e, hidx=hidx: e.tensor_tensor(
                                    f32(A_K2M + hidx, 1, 0, 1), f32(A_K2M + hidx, 1, 0, 1),
                                    f32(A_T16 + hidx, 1, 0, 1), op=MAXOP))

                if STOP in ("a1k", "a1ke") or (STOP == "a1ke0" and st == 0):
                    return None
                v_done = []
                for nt in range(4):
                    if nt % 2 == 0:
                        vs.wait(plan, "tensor", evac_v["pS1"][nt // 2])
                        ss.wait(plan, "tensor", evac_s["pS1"][nt // 2])
                    quarter = pS1[:, nt * 256:(nt + 1) * 256]
                    t_v = proj3(quarter, A_WVH, A_WVL,
                                lambda off, cc, nt=nt: x0h[:, cc, nt * 128:(nt + 1) * 128]
                                if off == A_WVH else x0l[:, cc, nt * 128:(nt + 1) * 128],
                                lambda cc: wsl[A_WVH][:, cc, :],
                                lambda cc: wsl[A_WVL][:, cc, :], True)
                    v_done.append(t_v)
                for bank in range(2):
                    ts.wait(plan, "vector", v_done[bank * 2 + 1])
                    jc0 = st * 4 + bank * 2
                    dst = sv_view[:, jc0:jc0 + 2, :, 0:64]
                    src = pS1[:, bank * 512:(bank + 1) * 512].rearrange(
                        "p (j h c) -> p j h c", j=2, h=GH)
                    v_ev = vs.inc(plan, "vector", lambda e, dst=dst, src=src:
                                  e.tensor_copy(dst, src))
                    evac_v["pS1"][bank] = v_ev
                    evac_s["pS1"][bank] = dep["ss"]

                if STOP == "a1v":
                    return None
                for p in range(NPAIR):
                    vs.wait(plan, "tensor", evac_v["pS0"][p])
                    ss.wait(plan, "tensor", evac_s["pS0"][p])
                    half = pS0[:, p * 512:(p + 1) * 512]
                    t_q = proj3(half, A_WQH, A_WQL,
                                lambda off, cc, p=p: wsl[off][:, cc, p * 128:(p + 1) * 128],
                                lambda cc: x0h[:, cc, :], lambda cc: x0l[:, cc, :], True)
                    ts.wait(plan, "scalar", max(t_q, sqsc_ts))
                    cslice = slice(p * 2048 + st * 512, p * 2048 + st * 512 + 512)
                    s_sq = ss.inc(plan, "scalar", lambda e, p=p: e.activation(
                        f32(A_SQSC, 512), pS0[:, p * 512:(p + 1) * 512],
                        AF.Square, bias=0.0, scale=1.0))
                    s_hi = ss.inc(plan, "scalar", lambda e, p=p, cslice=cslice: e.activation(
                        bf(A_SQHI, 4096)[:, cslice], pS0[:, p * 512:(p + 1) * 512],
                        AF.Copy, bias=0.0, scale=1.0))
                    evac_s["pS0"][p] = s_hi
                    ss.wait(plan, "vector", s_hi)
                    v_lo = vs.inc(plan, "vector", lambda e, p=p, cslice=cslice:
                                  e.scalar_tensor_tensor(
                                      bf(A_SQLO, 4096)[:, cslice], pS0[:, p * 512:(p + 1) * 512],
                                      1.0, bf(A_SQHI, 4096)[:, cslice], op0=MUL, op1=SUB))
                    evac_v["pS0"][p] = v_lo
                    ss.wait(plan, "tensor", s_sq)
                    vs.wait(plan, "tensor", stat_v_last)
                    mm(pB[0:1, 0:512], ones[0:64, 0:1], f32(A_SQSC, 512, 0, 64), True, True)
                    t_st = mm(pB[0:1, 512:1024], ones[64:128, 1:2], f32(A_SQSC, 512, 64, 128),
                              True, True, sem=ts)
                    sqsc_ts = t_st
                    ts.wait(plan, "vector", t_st)
                    for par in range(2):
                        hidx = 2 * p + par
                        src = pB[0:1, par * 512:par * 512 + 512]
                        stat_v_last = vs.inc(plan, "vector", lambda e, hidx=hidx, src=src, st=st:
                                             e.reduce_max(f32(A_Q2M + hidx * 4 + st, 1, 0, 1),
                                                          src, axis=mybir.AxisListType.X))

            vs.wait(plan, "vector", stat_v_last)
            for hidx in range(GH):
                plan.emit("vector", lambda e, hidx=hidx: e.tensor_scalar(
                    f32(A_T16 + hidx * 4, 4, 0, 1), f32(A_Q2M + hidx * 4, 4, 0, 1),
                    f32(A_K2M + hidx, 1, 0, 1), 0.0, op0=MUL, op1=ADD))
            v_t16 = vs.inc(plan, "vector", lambda e: e.tensor_copy(
                f32(A_T16 + 17, 1, 0, 1), f32(A_T16, 1, 0, 1)))
            vs.wait(plan, "scalar", v_t16)
            s_sqrt = ss.inc(plan, "scalar", lambda e: e.activation(
                f32(A_T16, 16, 0, 1), f32(A_T16, 16, 0, 1), AF.Sqrt, bias=0.0, scale=1.0))
            ss.wait(plan, "vector", s_sqrt)
            v_neg = vs.inc(plan, "vector", lambda e: e.tensor_scalar(
                f32(A_T16, 16, 0, 1), f32(A_T16, 16, 0, 1), -BMARG_M, -BMARG_A,
                op0=MUL, op1=ADD))
            vs.wait(plan, "tensor", v_neg)
            t_nb = mm(pB[:, 0:16], ones[0:1, 0:128], f32(A_T16, 16, 0, 1), True, True, sem=ts)
            ts.wait(plan, "scalar", t_nb)
            s_nb = ss.inc(plan, "scalar", lambda e: e.activation(
                f32(A_NEGB, 16), pB[:, 0:16], AF.Copy, bias=0.0, scale=1.0))
            return {"negb_s": s_nb, "kq_v": evac_v, "kq_s": evac_s,
                    "x0_ts": x0_tr_last, "pB_ts": t_nb}

        def stage_A2(s, g, a1):
            kh = bf(A_SKHI, 4096)
            kl = bf(A_SKLO, 4096)
            qh = bf(A_SQHI, 4096)
            ql = bf(A_SQLO, 4096)
            pS = [pS0, pS1]
            NB = NIT * NPAIR * NJC
            exp_done = {}
            pv_done = {}
            sfill_done = {}
            ss.wait(plan, "tensor", a1["negb_s"])
            state = {"oe_v": dep["vs"], "oe_dgp": dep["dgp"], "pb_s": a1["negb_s"]}

            def coords(b):
                t, jc = divmod(b, NJC)
                it, p = divmod(t, NPAIR)
                return it, p, jc

            def emit_S(b):
                it, p, jc = coords(b)
                sset = pS[b % 2]
                if b >= 2:
                    ss.wait(plan, "tensor", exp_done[b - 2])
                kcol = slice(p * 2048 + jc * 128, p * 2048 + jc * 128 + 128)
                icol = slice(p * 2048 + it * 512, p * 2048 + it * 512 + 512)
                passes = [(kh, qh, True, False), (kl, qh, False, False),
                          (kh, ql, False, True)]
                for (kk, qq, fst, lst) in passes:
                    mm(sset[:, 0:512], kk[0:64, kcol], qq[0:64, icol],
                       fst, lst, tp=(0, 0))
                    sem = ts if lst else None
                    r = mm(sset[:, 512:1024], kk[64:128, kcol], qq[64:128, icol],
                           fst, lst, tp=(64, 0), sem=sem)
                    if lst:
                        sfill_done[b] = r

            def emit_exp(b):
                it, p, jc = coords(b)
                sset = pS[b % 2]
                ts.wait(plan, "scalar", sfill_done[b])
                if b - RING >= 0:
                    ts.wait(plan, "scalar", pv_done[b - RING])
                rb = ring[b % RING]
                nbe = f32(A_NEGB + (2 * p) * 4 + it, 1)
                nbo = f32(A_NEGB + (2 * p + 1) * 4 + it, 1)
                plan.emit("scalar", lambda e, rb=rb, sset=sset, nbe=nbe: e.activation(
                    rb[:, 0:512], sset[:, 0:512], AF.Exp, bias=nbe, scale=1.0))
                exp_done[b] = ss.inc(plan, "scalar",
                                     lambda e, rb=rb, sset=sset, nbo=nbo: e.activation(
                                         rb[:, 512:1024], sset[:, 512:1024],
                                         AF.Exp, bias=nbo, scale=1.0))

            def emit_PV(b):
                it, p, jc = coords(b)
                ss.wait(plan, "tensor", exp_done[b])
                if jc == 0:
                    vs.wait(plan, "tensor", state["oe_v"])
                rb = ring[b % RING]
                voff_e = A_SV + jc * 260 + (2 * p) * 65
                voff_o = A_SV + jc * 260 + (2 * p + 1) * 65
                mm(pVe[:, :], f32(voff_e, 65), rb[:, 0:512], jc == 0, jc == NJC - 1)
                pv_done[b] = mm(pVo[:, :], f32(voff_o, 65), rb[:, 512:1024],
                                jc == 0, jc == NJC - 1, sem=ts)

            def emit_Oevac(t):
                it, p = divmod(t, NPAIR)
                ts.wait(plan, "vector", pv_done[(t + 1) * NJC - 1])
                rece = f32(A_X0TILE, 512, 64, 65)
                reco = f32(A_X0TILE + 512, 512, 64, 65)
                plan.emit("vector", lambda e, rece=rece: e.reciprocal(rece, pVe[64:65, :]))
                v_rec = vs.inc(plan, "vector", lambda e, reco=reco: e.reciprocal(
                    reco, pVo[64:65, :]))
                vs.wait(plan, "tensor", v_rec)
                ss.wait(plan, "tensor", state["pb_s"])
                mm(pB[0:64, 0:512], ones[64:65, 0:64], rece, True, True)
                t_bc = mm(pB[0:64, 512:1024], ones[64:65, 0:64], reco, True, True, sem=ts)
                ts.wait(plan, "scalar", t_bc)
                srb = f32(A_X0TILE, 1024, 0, 64)
                state["pb_s"] = ss.inc(plan, "scalar", lambda e, srb=srb: e.activation(
                    srb, pB[0:64, 0:1024], AF.Copy, bias=0.0, scale=1.0))
                ss.wait(plan, "vector", state["pb_s"])
                ohfe = f32(S_SWFB, 512, 0, 64)
                ohfo = f32(S_SWFB + 512, 512, 0, 64)
                d_gp.wait(plan, "vector", state["oe_dgp"])
                plan.emit("vector", lambda e, ohfe=ohfe, srb=srb: e.tensor_tensor(
                    ohfe, pVe[0:64, :], srb[:, 0:512], op=MUL))
                plan.emit("vector", lambda e, ohfo=ohfo, srb=srb: e.tensor_tensor(
                    ohfo, pVo[0:64, :], srb[:, 512:1024], op=MUL))
                he_hc, ho_hc = (g * GH + 2 * p) // 2, (g * GH + 2 * p + 1) // 2
                ecol = slice(he_hc * 2048 + it * 512, he_hc * 2048 + it * 512 + 512)
                plan.emit("vector", lambda e, ohfe=ohfe, ecol=ecol: e.tensor_copy(
                    bf(A_SOH, 16384, 0, 64)[:, ecol], ohfe))
                plan.emit("vector", lambda e, ohfe=ohfe, ecol=ecol: e.scalar_tensor_tensor(
                    bf(A_SOL, 16384, 0, 64)[:, ecol], ohfe, 1.0,
                    bf(A_SOH, 16384, 0, 64)[:, ecol], op0=MUL, op1=SUB))
                plan.emit("vector", lambda e, ohfo=ohfo: e.tensor_copy(
                    bf(A_OSTG, 512, 0, 64), ohfo))
                state["oe_v"] = vs.inc(plan, "vector", lambda e, ohfo=ohfo:
                                       e.scalar_tensor_tensor(
                                           bf(A_OSTG + 256, 512, 0, 64), ohfo, 1.0,
                                           bf(A_OSTG, 512, 0, 64), op0=MUL, op1=SUB))
                vs.wait(plan, "gpsimd", state["oe_v"])
                ocol = slice(ho_hc * 2048 + it * 512, ho_hc * 2048 + it * 512 + 512)
                dma_gp(lambda e, ocol=ocol: e.dma_start(
                    out=bf(A_SOH, 16384, 64, 128)[:, ocol],
                    in_=bf(A_OSTG, 512, 0, 64)))
                state["oe_dgp"] = dma_gp(lambda e, ocol=ocol: e.dma_start(
                    out=bf(A_SOL, 16384, 64, 128)[:, ocol],
                    in_=bf(A_OSTG + 256, 512, 0, 64)))

            for b in range(NB):
                emit_S(b)
                emit_exp(b)
                if b >= 1:
                    emit_PV(b - 1)
                    if (b - 1) % NJC == NJC - 1:
                        emit_Oevac((b - 1) // NJC)
            emit_PV(NB - 1)
            emit_Oevac((NB - 1) // NJC)
            return {"pv_ts": ts.n, "oe_v": state["oe_v"], "oe_dgp": state["oe_dgp"],
                    "exp_ss": ss.n}

        def stage_B(s, a2):
            fwait = {ts: dep["b_w_ts"], ss: dep["b_w_ss"], vs: dep["b_w_vs"]}
            wo_s, wo_v = split_weight(
                B_WOH, B_WOL,
                lambda k: w_d["woT"][(k // 2) * 128:(k // 2) * 128 + 128,
                                     (k % 2) * 512:(k % 2) * 512 + 512],
                2 * NCC, 512, S_SWFB, 1.0, fwait)
            vs.wait(plan, "tensor", a2["oe_v"])
            d_gp.wait(plan, "tensor", a2["oe_dgp"])
            if STOP == "bo":
                vs.wait(plan, "vector", a2["oe_v"])
                d_gp.wait(plan, "vector", a2["oe_dgp"])
                prev_dma = 0
                for ci in range(16):
                    d_gp.wait(plan, "vector", prev_dma)
                    v_r = vs.inc(plan, "vector", lambda e, ci=ci: e.tensor_tensor(
                        f32(B_X0TILE, 1024), bf(A_SOH + ci * 512, 1024),
                        bf(A_SOL + ci * 512, 1024), op=ADD))
                    vs.wait(plan, "gpsimd", v_r)
                    prev_dma = dma_gp(lambda e, ci=ci: e.dma_start(
                        out=out_d[ci * 128:(ci + 1) * 128, :],
                        in_=f32(B_X0TILE, 1024)))
                return {"x1t_v": vs.n, "wout_ts": ts.n, "ln_ss": ss.n}
            ss.wait(plan, "tensor", wo_s)
            vs.wait(plan, "tensor", wo_v)
            x1tv = f32(A_X1T, NCC * 2049).rearrange("p (c n) -> p c n", c=NCC)
            ts.wait(plan, "vector", a2["pv_ts"])
            v_z = vs.inc(plan, "vector", lambda e: e.memset(x1tv[:, :, 0:1], 0.0))

            soh = bf(A_SOH, 16384)
            sol = bf(A_SOL, 16384)
            woh = bf(B_WOH, 8192)
            wol = bf(B_WOL, 8192)
            prev_ev = [dep["vs"], dep["vs"]]
            prev_tr_v = v_z
            prev_x0_ts = dep["ts"]
            slot_ev = [a2["oe_v"], a2["oe_v"]]
            x1t_evs = []
            for isub in range(16):
                slot = isub % 2
                if isub < 2:
                    ts.wait(plan, "sync", a2["pv_ts"])
                vs.wait(plan, "sync", slot_ev[slot])
                ld = dma_in(f32(B_X0TILE + slot * 1024, 1024),
                            x_d[s * T + isub * 128: s * T + isub * 128 + 128, :])
                wout_done = []
                for ch in range(2):
                    vs.wait(plan, "tensor", prev_ev[ch])
                    half = pS0[:, ch * 512:(ch + 1) * 512]
                    entries = []
                    for hc in range(NCC):
                        lh = soh[:, hc * 2048 + isub * 128: hc * 2048 + isub * 128 + 128]
                        ll = sol[:, hc * 2048 + isub * 128: hc * 2048 + isub * 128 + 128]
                        rh = woh[:, hc * 1024 + ch * 512: hc * 1024 + ch * 512 + 512]
                        rl = wol[:, hc * 1024 + ch * 512: hc * 1024 + ch * 512 + 512]
                        entries += [(lh, rh), (lh, rl), (ll, rh)]
                    for i, (l, r) in enumerate(entries):
                        sem = ts if i == len(entries) - 1 else None
                        rr = mm(half, l, r, i == 0, i == len(entries) - 1, sem=sem)
                        if sem:
                            wout_done.append(rr)
                prev_x0_ts = wout_done[-1]
                d_in.wait(plan, "vector", ld)
                for ch in range(2):
                    ts.wait(plan, "vector", wout_done[ch])
                    plan.emit("vector", lambda e, ch=ch: e.tensor_tensor(
                        f32(B_Y0 + ch * 512, 512), pS0[:, ch * 512:(ch + 1) * 512],
                        f32(A_BOUTB + ch * 512, 512), op=ADD))
                    ev = vs.inc(plan, "vector", lambda e, ch=ch, slot=slot: e.tensor_tensor(
                        f32(B_Y0 + ch * 512, 512), f32(B_Y0 + ch * 512, 512),
                        f32(B_X0TILE + slot * 1024 + ch * 512, 512), op=ADD))
                    prev_ev[ch] = ev
                slot_ev[slot] = prev_ev[1]
                if STOP == "by0":
                    vs.wait(plan, "gpsimd", prev_ev[1])
                    dma_gp(lambda e, s=s, isub=isub: e.dma_start(
                        out=out_d[s * T + isub * 128: s * T + isub * 128 + 128, :],
                        in_=f32(B_Y0, 1024)))
                s_ln, v_ln = ln_norm(f32(B_Y0, 1024), f32(B_SQSC, 1024), B_LN,
                                     f32(B_X1TILE, 1024), prev_ev[1])
                if STOP == "bx1":
                    vs.wait(plan, "gpsimd", v_ln)
                    dma_gp(lambda e, s=s, isub=isub: e.dma_start(
                        out=out_d[s * T + isub * 128: s * T + isub * 128 + 128, :],
                        in_=f32(B_X1TILE, 1024)))
                vs.wait(plan, "tensor", v_ln)
                ts.wait(plan, "tensor", a2["pv_ts"])
                vs.wait(plan, "tensor", prev_tr_v)
                tr = None
                for cc in range(NCC):
                    sem = ts if cc == NCC - 1 else None
                    tr = mm(pS1[:, cc * 128:(cc + 1) * 128], f32(B_X1TILE + cc * 128, 128),
                            ident, True, True, sem=sem) or tr
                ts.wait(plan, "vector", tr)
                srcv = pS1[:, 0:1024].rearrange("p (c n) -> p c n", c=NCC)
                prev_tr_v = vs.inc(plan, "vector", lambda e, isub=isub, srcv=srcv: e.tensor_copy(
                    x1tv[:, :, 1 + isub * 128: 1 + isub * 128 + 128], srcv))
                x1t_evs.append(prev_tr_v)
            return {"x1t_v": prev_tr_v, "wout_ts": prev_x0_ts, "ln_ss": ss.n}

        def stage_C(s, b_res):
            fw1 = {ts: b_res["wout_ts"], ss: b_res["ln_ss"], vs: b_res["x1t_v"]}

            def wsrc(name):
                return lambda k: w_d[name][(k // 2) * 128:(k // 2) * 128 + 128,
                                           (k % 2) * 512:(k % 2) * 512 + 512]
            wr_s, wr_v = split_weight(C_WRH, C_WRL, wsrc("wrT"),
                                      2 * NCC, 512, S_SWFB, 1.0, fw1)
            k2_s, k2_v = split_weight(C_WK2H, C_WK2L, wsrc("wk2T"),
                                      2 * NCC, 512, S_SWFB, 1.0, {vs: wr_v})
            w_s, w_v = split_weight(C_WV2H, C_WV2L, wsrc("wv2T"),
                                    2 * NCC, 512, S_SWFB, 1.0, {vs: k2_v})
            ss.wait(plan, "tensor", w_s)
            vs.wait(plan, "tensor", w_v)
            ss.wait(plan, "vector", w_s)
            vs.wait(plan, "scalar", w_v)

            x1tv = f32(A_X1T, NCC * 2049).rearrange("p (c n) -> p c n", c=NCC)
            wrh = bf(C_WRH, 8192)
            wrl = bf(C_WRL, 8192)
            wkh = bf(C_WK2H, 8192)
            wkl = bf(C_WK2L, 8192)
            wvh = bf(C_WV2H, 8192)
            wvl = bf(C_WV2L, 8192)
            xh = bf(C_XH, 2048).rearrange("p (c n) -> p c n", c=NCC)
            xl = bf(C_XL, 2048).rearrange("p (c n) -> p c n", c=NCC)
            kah = bf(C_KAH, 2048).rearrange("p (c n) -> p c n", c=NCC)
            kal = bf(C_KAL, 2048).rearrange("p (c n) -> p c n", c=NCC)

            prev_mix_ts = b_res["wout_ts"]
            prev_proj_ev = {"pS0": [dep["vs"], dep["vs"]], "pS1": [dep["vs"], dep["vs"]]}
            prev_proj_es = {"pS0": [b_res["ln_ss"], b_res["ln_ss"]],
                            "pS1": [b_res["ln_ss"], b_res["ln_ss"]]}
            prev_out_v = dep["vs"]
            prev_out_dgp = d_gp.n
            kev_v = dep["vs"]

            def mixes(tt_, mu_off, omu_off):
                nonlocal prev_mix_ts
                res_s = res_v = None
                ts.wait(plan, "scalar", prev_mix_ts)
                ts.wait(plan, "vector", prev_mix_ts)
                for cc in range(NCC):
                    xprev = x1tv[:, cc, tt_ * 256: tt_ * 256 + 256]
                    xcur = x1tv[:, cc, 1 + tt_ * 256: 1 + tt_ * 256 + 256]
                    s_t = ss.inc(plan, "scalar", lambda e, xprev=xprev, cc=cc: e.activation(
                        f32(S_SWFB, 256), xprev, AF.Copy, bias=0.0,
                        scale=f32(omu_off + cc, 1)))
                    ss.wait(plan, "vector", s_t)
                    v_m = vs.inc(plan, "vector", lambda e, xcur=xcur, cc=cc: e.scalar_tensor_tensor(
                        f32(S_SWFB + 256, 256), xcur, f32(mu_off + cc, 1),
                        f32(S_SWFB, 256), op0=MUL, op1=ADD))
                    vs.wait(plan, "scalar", v_m)
                    res_s = ss.inc(plan, "scalar", lambda e, cc=cc: e.activation(
                        xh[:, cc, :], f32(S_SWFB + 256, 256), AF.Copy, bias=0.0, scale=1.0))
                    ss.wait(plan, "vector", res_s)
                    res_v = vs.inc(plan, "vector", lambda e, cc=cc: e.scalar_tensor_tensor(
                        xl[:, cc, :], f32(S_SWFB + 256, 256), 1.0, xh[:, cc, :],
                        op0=MUL, op1=SUB))
                return res_s, res_v

            def ffn_proj(wh, wl, psum, bank_evac, ready_s, ready_v):
                nonlocal prev_mix_ts
                ss.wait(plan, "tensor", ready_s)
                vs.wait(plan, "tensor", ready_v)
                done = {}
                for dc in range(NCC):
                    bank = (dc // 2) % 2
                    reg = psum[:, (dc % 4) * 256:(dc % 4) * 256 + 256]
                    if dc % 2 == 0:
                        key = "pS0" if psum is pS0 else "pS1"
                        vs.wait(plan, "tensor", prev_proj_ev[key][bank])
                        ss.wait(plan, "tensor", prev_proj_es[key][bank])
                    entries = []
                    for cc in range(NCC):
                        lh = wh[:, cc * 1024 + dc * 128: cc * 1024 + dc * 128 + 128]
                        ll = wl[:, cc * 1024 + dc * 128: cc * 1024 + dc * 128 + 128]
                        entries += [(lh, xh[:, cc, :]), (lh, xl[:, cc, :]), (ll, xh[:, cc, :])]
                    for i, (l, r) in enumerate(entries):
                        sem = ts if i == len(entries) - 1 else None
                        rr = mm(reg, l, r, i == 0, i == len(entries) - 1, sem=sem)
                        if sem:
                            done[dc] = rr
                    if dc % 2 == 1:
                        bank_evac((dc % 4) // 2, dc - 1, done[dc], psum)
                    if dc == NCC - 1:
                        prev_mix_ts = done[dc]
                return done

            NST = 8
            for tt_ in range(NST):
                mk_s, mk_v = mixes(tt_, A_MUK, A_OMUK)

                def k_evac(bank, dc0, t_done, psum):
                    nonlocal kev_v
                    ts.wait(plan, "scalar", t_done)
                    vs.wait(plan, "scalar", kev_v)
                    s_r = ss.inc(plan, "scalar", lambda e, bank=bank: e.activation(
                        f32(S_SWFB, 512), psum[:, bank * 512:(bank + 1) * 512],
                        AF.Relu, bias=0.0, scale=1.0))
                    s_q = ss.inc(plan, "scalar", lambda e: e.activation(
                        f32(S_SWFB + 512, 512), f32(S_SWFB, 512), AF.Square,
                        bias=0.0, scale=1.0))
                    src = f32(S_SWFB + 512, 512).rearrange("p (c n) -> p c n", c=2)
                    s_h = ss.inc(plan, "scalar", lambda e, dc0=dc0, src=src: e.activation(
                        kah[:, dc0:dc0 + 2, :], src, AF.Copy, bias=0.0, scale=1.0))
                    prev_proj_es["pS0"][bank] = s_h
                    ss.wait(plan, "vector", s_h)
                    v_l = vs.inc(plan, "vector", lambda e, dc0=dc0, src=src:
                                 e.scalar_tensor_tensor(kal[:, dc0:dc0 + 2, :], src, 1.0,
                                                        kah[:, dc0:dc0 + 2, :],
                                                        op0=MUL, op1=SUB))
                    prev_proj_ev["pS0"][bank] = v_l
                    kev_v = v_l
                ffn_proj(wkh, wkl, pS0, k_evac, mk_s, mk_v)

                mr_s, mr_v = mixes(tt_, A_MUR, A_OMUR)

                def r_evac(bank, dc0, t_done, psum):
                    ts.wait(plan, "scalar", t_done)
                    s_g = ss.inc(plan, "scalar", lambda e, bank=bank, dc0=dc0: e.activation(
                        f32(C_RST + dc0 * 256, 512), psum[:, bank * 512:(bank + 1) * 512],
                        AF.Sigmoid, bias=0.0, scale=1.0))
                    prev_proj_es["pS1"][bank] = s_g
                    prev_proj_ev["pS1"][bank] = dep["vs"]
                ffn_proj(wrh, wrl, pS1, r_evac, mr_s, mr_v)

                sig_ss = ss.n

                def kv_evac(bank, dc0, t_done, psum):
                    ts.wait(plan, "vector", t_done)
                    ss.wait(plan, "vector", sig_ss)
                    plan.emit("vector", lambda e, bank=bank, dc0=dc0: e.tensor_tensor(
                        f32(C_RST + dc0 * 256, 512), psum[:, bank * 512:(bank + 1) * 512],
                        f32(C_RST + dc0 * 256, 512), op=MUL))
                    x1sl = x1tv[:, dc0:dc0 + 2, 1 + tt_ * 256: 1 + tt_ * 256 + 256]
                    ysl = f32(C_RST + dc0 * 256, 512).rearrange("p (c n) -> p c n", c=2)
                    v_y = vs.inc(plan, "vector", lambda e, ysl=ysl, x1sl=x1sl: e.tensor_tensor(
                        ysl, ysl, x1sl, op=ADD))
                    prev_proj_ev["pS0"][bank] = v_y
                    prev_proj_es["pS0"][bank] = sig_ss

                xh_s, xl_s = xh, xl

                def ffn_proj_kv():
                    nonlocal prev_mix_ts
                    ss.wait(plan, "tensor", max(prev_proj_es["pS0"]))
                    vs.wait(plan, "tensor", max(prev_proj_ev["pS0"]))
                    done = {}
                    for dc in range(NCC):
                        bank = (dc // 2) % 2
                        reg = pS0[:, (dc % 4) * 256:(dc % 4) * 256 + 256]
                        if dc % 2 == 0 and dc >= 2:
                            vs.wait(plan, "tensor", prev_proj_ev["pS0"][bank])
                            ss.wait(plan, "tensor", prev_proj_es["pS0"][bank])
                        entries = []
                        for cc in range(NCC):
                            lh = wvh[:, cc * 1024 + dc * 128: cc * 1024 + dc * 128 + 128]
                            ll = wvl[:, cc * 1024 + dc * 128: cc * 1024 + dc * 128 + 128]
                            entries += [(lh, kah[:, cc, :]), (lh, kal[:, cc, :]),
                                        (ll, kah[:, cc, :])]
                        for i, (l, r) in enumerate(entries):
                            sem = ts if i == len(entries) - 1 else None
                            rr = mm(reg, l, r, i == 0,
                                    i == len(entries) - 1, sem=sem)
                            if sem:
                                done[dc] = rr
                        if dc % 2 == 1:
                            kv_evac((dc % 4) // 2, dc - 1, done[dc], pS0)
                    return done
                ffn_proj_kv()

                ysv = f32(C_RST, 2048).rearrange("p (c n) -> p c n", c=NCC)
                for nn in range(2):
                    vs.wait(plan, "tensor", prev_proj_ev["pS0"][0])
                    vs.wait(plan, "tensor", prev_proj_ev["pS0"][1])
                    ss.wait(plan, "tensor", prev_proj_es["pS1"][0])
                    ss.wait(plan, "tensor", prev_proj_es["pS1"][1])
                    vs.wait(plan, "tensor", prev_out_v)
                    tr = None
                    for cc in range(NCC):
                        sem = ts if cc == NCC - 1 else None
                        tr = mm(pS1[:, cc * 128:(cc + 1) * 128],
                                ysv[:, cc, nn * 128: nn * 128 + 128],
                                ident, True, True, sem=sem) or tr
                    ts.wait(plan, "vector", tr)
                    v_y1 = vs.inc(plan, "vector", lambda e: e.tensor_copy(
                        f32(C_Y1TILE, 1024), pS1[:, 0:1024]))
                    prev_proj_ev["pS1"] = [v_y1, v_y1]
                    prev_proj_es["pS1"] = [dep["ss"], dep["ss"]]
                    d_gp.wait(plan, "scalar", prev_out_dgp)
                    d_gp.wait(plan, "vector", prev_out_dgp)
                    s_ln, v_ln = ln_norm(f32(C_Y1TILE, 1024), f32(C_X2TILE, 1024), S_LN,
                                         f32(C_X2TILE, 1024), v_y1)
                    prev_out_v = v_ln
                    vs.wait(plan, "gpsimd", v_ln)
                    row = s * T + tt_ * 256 + nn * 128
                    prev_out_dgp = dma_gp(lambda e, row=row: e.dma_start(
                        out=out_d[row:row + 128, :], in_=f32(C_X2TILE, 1024)))
            return {}

        done = False
        for s in range(0 if STOP == "pre" else SEQS):
            a2 = None
            for g in range(NG):
                a1 = stage_A1(s, g)
                if STOP.startswith("a1") and s == 0 and g == 0:
                    done = True
                    break
                a2g = stage_A2(s, g, a1)
                dep["ts"] = a2g["pv_ts"]
                dep["ss"] = a2g["exp_ss"]
                dep["vs"] = a2g["oe_v"]
                dep["dgp"] = a2g["oe_dgp"]
                a2 = a2g
                if STOP == "a2" and s == 0 and g == 0:
                    done = True
                    break
            if done:
                break
            dep["b_w_ts"] = dep["ts"]
            dep["b_w_ss"] = dep["ss"]
            dep["b_w_vs"] = dep["vs"]
            b_res = stage_B(s, a2)
            if STOP in ("b", "bx1", "bo", "by0") and s == 0:
                break
            stage_C(s, b_res)
            if STOP == "c" and s == 0:
                break
            dep["ts"] = ts.n
            dep["ss"] = ss.n
            dep["vs"] = vs.n

        plan.emit("gpsimd", lambda e: e.wait_ge(d_gp_h, d_gp.n))

        plan.run(block)
    return nc


def kernel(L0_attn_x, L0_attn_Wq, L0_attn_Wk, L0_attn_Wv, L0_attn_Wout, L0_attn_bout,
           L0_ffn_W_r, L0_ffn_W_k, L0_ffn_W_v, L0_ffn_mu_r, L0_ffn_mu_k):
    x = np.ascontiguousarray(np.asarray(L0_attn_x, dtype=np.float32))
    wT = {
        "wqT": np.ascontiguousarray(np.asarray(L0_attn_Wq, np.float32).T),
        "wkT": np.ascontiguousarray(np.asarray(L0_attn_Wk, np.float32).T),
        "wvT": np.ascontiguousarray(np.asarray(L0_attn_Wv, np.float32).T),
        "woT": np.ascontiguousarray(np.asarray(L0_attn_Wout, np.float32).T),
        "wrT": np.ascontiguousarray(np.asarray(L0_ffn_W_r, np.float32).T),
        "wk2T": np.ascontiguousarray(np.asarray(L0_ffn_W_k, np.float32).T),
        "wv2T": np.ascontiguousarray(np.asarray(L0_ffn_W_v, np.float32).T),
    }
    bout = np.ascontiguousarray(np.asarray(L0_attn_bout, np.float32).reshape(1, C))
    mus = np.zeros((128, 16), np.float32)
    mus[:, 0:8] = np.asarray(L0_ffn_mu_r, np.float32).reshape(8, 128).T
    mus[:, 8:16] = np.asarray(L0_ffn_mu_k, np.float32).reshape(8, 128).T
    consts = np.zeros((128, 256), np.float32)
    consts[:, 0:128] = 1.0
    consts[:, 128:256] = np.eye(128, dtype=np.float32)

    nc = build_nc()
    in_maps = []
    for core in range(NCORES):
        shard = np.ascontiguousarray(x[core * SEQS:(core + 1) * SEQS].reshape(SEQS * T, C))
        m = {"x": shard, "bout": bout, "mus": mus, "consts": consts}
        m.update(wT)
        in_maps.append(m)
    res = run_bass_kernel_spmd(nc, in_maps, core_ids=list(range(NCORES)))
    out = np.concatenate([r["out"].reshape(SEQS, T, C) for r in res.results], axis=0)
    return out
